# version 06 (final)
# baseline (speedup 1.0000x reference)
import numpy as np
import ml_dtypes
from contextlib import ExitStack

import concourse.bass as bass
import concourse.mybir as mybir
from concourse.bass_utils import run_bass_kernel_spmd
from concourse.aot_env import allow_ici_all_to_all

F32 = mybir.dt.float32
BF16 = mybir.dt.bfloat16
AF = mybir.ActivationFunctionType
ALU = mybir.AluOpType
AX = mybir.AxisListType

NCORE = 8
SEQ = 8192
DM = 4096
TLOC = 1024
CAP = 3072.0
LN_EPS = 1e-5
RMS_EPS = 1e-5
INV_SQRT_D = float(1.0 / np.sqrt(128.0))
SOFT_B = 40.0
P40 = float(2.0 ** 40)


def _pack_w(w_io):
    I, O = w_io.shape
    a = np.ascontiguousarray(w_io, dtype=np.float32)
    a = a.reshape(I // 128, 128, O // 128, 128).transpose(2, 1, 0, 3)
    return np.ascontiguousarray(a.reshape(O // 128, 128, I))


def _pack_small(w_io):
    I, O = w_io.shape
    a = np.ascontiguousarray(w_io, dtype=np.float32)
    a = a.reshape(I // 128, 128, O).transpose(1, 0, 2)
    return np.ascontiguousarray(a.reshape(128, (I // 128) * O))


def _rope_tables():
    half = 64
    inv = 1.0 / (10000.0 ** (np.arange(half, dtype=np.float64) / half))
    ang = np.arange(SEQ, dtype=np.float64)[:, None] * inv[None, :]
    c = np.cos(ang).T
    s = np.sin(ang).T
    return (np.concatenate([c, c], axis=0).astype(np.float32),
            np.concatenate([-s, s], axis=0).astype(np.float32))


class Prog:
    ENGS = ("tensor", "vector", "scalar", "gpsimd", "sync")

    def __init__(self, nc, stack):
        self.nc = nc
        self.sem = {}
        for name in ("s_tensor", "s_vector", "s_scalar", "s_gpsimd",
                     "s_dma", "s_gdma", "s_cc",
                     "s_dw0", "s_dw1", "s_dg", "s_dk", "s_dv", "s_dq",
                     "s_dc0", "s_dc1"):
            self.sem[name] = stack.enter_context(nc.semaphore(name))
        self.cnt = {k: 0 for k in self.sem}
        self.esem = {"tensor": "s_tensor", "vector": "s_vector",
                     "scalar": "s_scalar", "gpsimd": "s_gpsimd"}
        self.steps = {e: [] for e in self.ENGS}
        self.observed = {e: {} for e in self.ENGS}
        self.last = None

    DMASEMS = ("s_dma", "s_gdma", "s_dw0", "s_dw1", "s_dg", "s_dk", "s_dv",
               "s_dq", "s_dc0", "s_dc1")

    def _add_wait(self, eng, ev):
        if ev is None:
            return
        s, v = ev
        if s in self.DMASEMS and v < self.cnt[s]:
            v = self.cnt[s]
        if self.observed[eng].get(s, 0) >= v:
            return
        self.observed[eng][s] = v
        self.steps[eng].append(("w", s, v))

    def step(self, eng, fn, waits=None, dmas=0, sem=None, incs=1):
        evs = waits if waits is not None else ([self.last] if self.last else [])
        for ev in evs:
            self._add_wait(eng, ev)
        if dmas:
            sname = sem or ("s_dma" if eng == "sync" else "s_gdma")
            inc = 16 * dmas
        else:
            sname = self.esem[eng]
            inc = incs
        self.cnt[sname] += inc
        self.steps[eng].append(("d", fn, sname, dmas, incs))
        self.last = (sname, self.cnt[sname])
        return self.last

    def cc(self, fn, waits=None):
        evs = waits if waits is not None else ([self.last] if self.last else [])
        for ev in evs:
            self._add_wait("gpsimd", ev)
        self.cnt["s_cc"] += 16
        val = self.cnt["s_cc"]
        self.steps["gpsimd"].append(("c", fn, "s_cc", val))
        self.last = ("s_cc", val)
        return self.last

    def event(self):
        return self.last

    def finish(self):
        for s in self.DMASEMS:
            if s != "s_gdma" and self.cnt[s]:
                self.steps["sync"].append(("w", s, self.cnt[s]))
        self.steps["sync"].append(("n",))

    def build(self, block):
        sems = self.sem

        def runner(eng_name):
            def run(eng):
                for item in self.steps[eng_name]:
                    k = item[0]
                    if k == "w":
                        eng.wait_ge(sems[item[1]], item[2])
                    elif k == "n":
                        eng.nop()
                    elif k == "d":
                        _, fn, sname, dmas, incs = item
                        out = fn(eng)
                        instrs = out if isinstance(out, (list, tuple)) else [out]
                        if dmas:
                            assert len(instrs) == dmas
                            for ins in instrs:
                                ins.then_inc(sems[sname], 16)
                        elif incs > 1:
                            assert len(instrs) == incs, (len(instrs), incs)
                            for ins in instrs:
                                ins.then_inc(sems[sname], 1)
                        else:
                            instrs[-1].then_inc(sems[sname], 1)
                    else:
                        _, fn, sname, val = item
                        fn(eng).then_inc(sems[sname], 16)
                        eng.wait_ge(sems[sname], val)
            return run

        runners = {e: runner(e) for e in self.ENGS}

        @block.tensor
        def _(eng):
            runners["tensor"](eng)

        @block.vector
        def _(eng):
            runners["vector"](eng)

        @block.scalar
        def _(eng):
            runners["scalar"](eng)

        @block.gpsimd
        def _(eng):
            runners["gpsimd"](eng)

        @block.sync
        def _(eng):
            runners["sync"](eng)


class Arena:
    def __init__(self, tens, cols):
        self.t = tens
        self.cols = cols
        self.off = 0
        self.marks = []

    def alloc(self, n):
        assert self.off + n <= self.cols, f"arena overflow {self.off}+{n}>{self.cols}"
        ap = self.t[:, self.off:self.off + n]
        self.off += n
        return ap

    def push(self):
        self.marks.append(self.off)

    def pop(self):
        self.off = self.marks.pop()


def build_program(launch):
    nc = bass.Bass("TRN2", target_bir_lowering=False, debug=False)
    st = ExitStack()

    def ein(name, shape, dtype=F32):
        return nc.dram_tensor(name, list(shape), dtype, kind="ExternalInput")

    LM = launch
    xt = ein("xt", [DM, TLOC])
    def dt_chain(t0, n, XN, dt_f, scr, XF, AF32, BF32):
        for f in range(32):
            ld = dma(XF, xt[f * 128:(f + 1) * 128, t0:t0 + n], waits=[P.event()])
            P.step("vector", lambda e: e.tensor_mul(XF, XF, AF32), waits=[ld, P.event()])
            P.step("vector", lambda e: e.tensor_add(XF, XF, BF32))
            P.step("tensor", lambda e, f2=f: e.matmul(
                PSB[0:32, 0:n], WINF[:, f2 * 32:(f2 + 1) * 32][0:128, :], XF,
                start=(f2 == 0), stop=(f2 == 31), skip_group_check=True))
        P.step("scalar", lambda e: e.activation(scr[0:32, :], PSB[0:32, 0:n],
                                                AF.Identity, bias=SMALL[0:32, 0:1]))
        P.step("scalar", lambda e: e.activation(dt_f[0:32, :], scr[0:32, :], AF.Abs))
        P.step("scalar", lambda e: e.activation(dt_f[0:32, :], dt_f[0:32, :], AF.Exp, scale=CKON[0:32, 2:3]))
        P.step("scalar", lambda e: e.activation(dt_f[0:32, :], dt_f[0:32, :], AF.Ln, bias=CKON[0:32, 1:2]))
        P.step("scalar", lambda e: e.activation(scr[0:32, :], scr[0:32, :], AF.Relu))
        P.step("vector", lambda e: e.tensor_add(dt_f[0:32, :], dt_f[0:32, :],
                                                scr[0:32, :]))
        P.step("vector", lambda e: e.tensor_scalar(scr[0:32, :], dt_f[0:32, :],
                                                   SMALL[0:32, 1:2], None, ALU.mult))
        for cc in range(n // 128):
            g = scr[0:32, cc * 128:(cc + 1) * 128]
            d = GHM[0:32, t0 + cc * 128: t0 + (cc + 1) * 128]
            P.step("vector", lambda e, g2=g, d2=d: e.tensor_tensor_scan(
                d2, g2, ZERW[0:32, 0:128], 0.0, ALU.add, ALU.add))
    
    def chunk_gparts(t0c, DTT, need_bias=False):
        GTM = af.alloc(32)
        AUX = af.alloc(96)
        transp(PSB[:, 1056:1088], GHM[0:32, t0c:t0c + 128], C_IDF[0:32, 0:32])
        P.step("vector", lambda e, T=GTM: e.tensor_copy(T, PSB[:, 1056:1088]))
        transp(PSB[0:1, 1088:1120], GHM[0:32, t0c + 127:t0c + 128], C_IDF[0:32, 0:32])
        P.step("vector", lambda e: e.tensor_copy(ROWS[0:1, 96:128], PSB[0:1, 1088:1120]))
        mm1(PSB[:, 1120:1152], ONESF[0:1, :], ROWS[0:1, 96:128])
        P.step("vector", lambda e, X=AUX, T=GTM: e.tensor_sub(X[:, 64:96], PSB[:, 1120:1152], T))
        P.step("scalar", lambda e, X=AUX: e.activation(X[:, 0:32], X[:, 64:96], AF.Exp))
        P.step("scalar", lambda e, X=AUX: e.activation(X[:, 32:64], PSB[:, 1120:1152], AF.Exp))
        P.step("vector", lambda e, X=AUX, T=DTT: e.tensor_mul(X[:, 0:32], X[:, 0:32], T))
        out = {"GTM": GTM, "AUX": AUX}
        if need_bias:
            BIA = af.alloc(32)
            EXG = af.alloc(32)
            P.step("scalar", lambda e, B2=BIA, T=DTT: e.activation(B2, T, AF.Ln))
            P.step("vector", lambda e, B2=BIA: e.tensor_scalar(B2, B2, -60.0, None, ALU.max))
            P.step("vector", lambda e, B2=BIA, T=GTM: e.tensor_sub(B2, B2, T))
            P.step("scalar", lambda e, X=EXG, T=GTM: e.activation(X, T, AF.Exp))
            out["BIA"] = BIA
            out["EXG"] = EXG
        return out
    
    def chunk_states(KTM, VTM, AUX, KD, update_prevb):
        for h in range(32):
            kv = kvh_of(h)
            P.step("vector", lambda e, h2=h, k2=kv, D2=KD, T=KTM, X=AUX:
                   e.tensor_scalar(D2, T[:, k2 * 128:(k2 + 1) * 128],
                                   X[:, h2:h2 + 1], None, ALU.mult))
            mm1(PSB[:, 256:384], KD, VTM[:, kvh_of(h) * 128:(kvh_of(h) + 1) * 128])
            sl = SLOC[:, h * 128:(h + 1) * 128]
            P.step("vector", lambda e, h2=h, s=sl, X=AUX: e.tensor_scalar(
                s, s, X[:, 32 + h2:33 + h2], None, ALU.mult))
            P.step("vector", lambda e, s=sl: e.tensor_add(s, s, PSB[:, 256:384]))
            if update_prevb:
                P.step("vector", lambda e, h2=h, s=sl: e.tensor_copy(
                    PREVB[:, h2 * 128:(h2 + 1) * 128], s))
    
    if LM == "L1":
        wk_a = ein("wk_a", [8, 128, DM]); wv_a = ein("wv_a", [8, 128, DM])
        win_a = ein("win_a", [128, 1024])
        dtb = ein("dtb", [32, 1]); alog = ein("alog", [32, 1])
        acs = ein("acs", [128, TLOC]); asn = ein("asn", [128, TLOC])
        ka_d = nc.dram_tensor("ka_d", [1024, TLOC], BF16, kind="ExternalOutput")
        va_d = nc.dram_tensor("va_d", [TLOC, 1024], BF16, kind="ExternalOutput")
        dtt_d = nc.dram_tensor("dtt_d", [TLOC, 32], F32, kind="ExternalOutput")
        grow_d = nc.dram_tensor("grow_d", [1, 32768], F32, kind="ExternalOutput")
        agg_o = nc.dram_tensor("agg_o", [4104, 128], F32, kind="ExternalOutput")
        ghm_o = nc.dram_tensor("ghm_o", [32, TLOC], F32, kind="ExternalOutput")
        stats_o = nc.dram_tensor("stats_o", [1, 2048], F32, kind="ExternalOutput")
    if LM == "L2":
        wq_a = ein("wq_a", [32, 128, DM]); wg_a = ein("wg_a", [32, 128, DM])
        wo_a = ein("wo_a", [32, 128, DM])
        gnw = ein("gnw", [128, 1])
        acs = ein("acs", [128, TLOC]); asn = ein("asn", [128, TLOC])
        mbrep = ein("mbrep", [128, 1024], BF16)
        gvec = ein("gvec", [128, 8]); ivec = ein("ivec", [128, 8])
        ka_d = ein("ka_d", [1024, TLOC], BF16)
        va_d = ein("va_d", [TLOC, 1024], BF16)
        dtt_d = ein("dtt_d", [TLOC, 32], F32)
        grow_d = ein("grow_d", [1, 32768], F32)
        agg_all = ein("agg_all", [8 * 4104, 128])
        ghm_i = ein("ghm_i", [32, TLOC])
        stats_i = ein("stats_i", [1, 2048])
        x_o = nc.dram_tensor("x_o", [DM, TLOC], F32, kind="ExternalOutput")
    if LM in ("L3", "L6"):
        wm_f = ein("wm_f", [32, 128, DM]); ast_f = ein("ast_f", [128, DM])
        gw1 = ein("gw1", [128, 1024])
        gb1 = ein("gb1", [1, 32]); gb2 = ein("gb2", [1, 8])
        lng = ein("lng", [1, 32]); lnb = ein("lnb", [1, 32])
        gw2 = ein("gw2", [32, 8])
        cnt_o = nc.dram_tensor("cnt_o", [1, 16], F32, kind="ExternalOutput")
        tmp_o = nc.dram_tensor("tmp_o", [128, 1024], BF16, kind="ExternalOutput")
        eq_o = nc.dram_tensor("eq_o", [128, 128], BF16, kind="ExternalOutput")
        w12_o = nc.dram_tensor("w12_o", [128, 16], F32, kind="ExternalOutput")
        x_o = nc.dram_tensor("x_o", [DM, TLOC], F32, kind="ExternalOutput")
    if LM in ("L4", "L7"):
        bst_f = ein("bst_f", [128, DM])
        rep_f = ein("rep_f", [8, 128], BF16)
        cnt_all = ein("cnt_all", [8, 16])
        tmp_i = ein("tmp_i", [128, 1024], BF16)
        eq_i = ein("eq_i", [128, 128], BF16)
        w12_i = ein("w12_i", [128, 16])
    if LM == "L4":
        xn_o = nc.dram_tensor("xn_o", [DM, TLOC], BF16, kind="ExternalOutput")
        x_o = nc.dram_tensor("x_o", [DM, TLOC], F32, kind="ExternalOutput")
        stats_o = None
    if LM == "L5":
        wq_b = ein("wq_b", [4, 128, DM]); wk_b = ein("wk_b", [1, 128, DM])
        wv_b = ein("wv_b", [1, 128, DM])
        bq_b = ein("bq_b", [512, 1]); bk_b = ein("bk_b", [128, 1])
        bv_b = ein("bv_b", [128, 1])
        bcs = ein("bcs", [128, SEQ], BF16); bsn = ein("bsn", [128, SEQ], BF16)
        dmask = ein("dmask", [128, 2048], BF16)
        xn_all = ein("xn_all", [8 * DM, TLOC], BF16)
        kb_d = nc.dram_tensor("kb_d", [128, SEQ], BF16)
        vb_d = nc.dram_tensor("vb_d", [SEQ, 128], BF16)
        qb_d = nc.dram_tensor("qb_d", [512, SEQ], BF16)
        ctx_o = nc.dram_tensor("ctx_o", [DM, TLOC], BF16, kind="ExternalOutput")
    if LM == "L6":
        wo_b = ein("wo_b", [32, 128, DM])
        bo_b = ein("bo_b", [DM, 1])
        ctx_all = ein("ctx_all", [DM, TLOC], BF16)
    if LM == "L7":
        out_d = nc.dram_tensor("out", [DM, TLOC], F32, kind="ExternalOutput")
    id_bf = ein("id_bf", [128, 128], BF16)
    id_f32 = ein("id_f32", [128, 128])

    sb = lambda n, sh, dt: st.enter_context(nc.sbuf_tensor(n, list(sh), dt))
    XR = sb("XR", [128, 32 * TLOC], BF16)
    SLOC = sb("SLOC", [128, 4096], F32)
    PREVB = sb("PREVB", [128, 4096], BF16)
    WST = sb("WST", [128, 2048], F32)
    WBF = sb("WBF", [128, 8192], BF16)
    GHM = sb("GHM", [32, TLOC], F32)
    C_MB = sb("C_MB", [128, 1024], BF16)
    C_DM = sb("C_DM", [128, 2048], BF16)
    C_CS = sb("C_CS", [128, TLOC], BF16)
    C_SN = sb("C_SN", [128, TLOC], BF16)
    C_ID = sb("C_ID", [128, 128], BF16)
    C_IDF = sb("C_IDF", [128, 128], F32)
    C_REP = sb("C_REP", [8, 128], BF16)
    C_GV = sb("C_GV", [128, 8], F32)
    C_IV = sb("C_IV", [128, 8], F32)
    ONESB = sb("ONESB", [128, 1], BF16)
    ONESF = sb("ONESF", [1, 128], F32)
    ONESF40 = sb("ONESF40", [1, 128], F32)
    GNC = sb("GNC", [128, 1], F32)
    SMALL = sb("SMALL", [128, 16], F32)
    CKON = sb("CKON", [128, 8], F32)
    ROWS = sb("ROWS", [1, 1536], F32)
    GLN = sb("GLN", [128, 64], F32)
    G2B = sb("G2B", [32, 8], BF16)
    GRW = sb("GRW", [1, 1024], F32)
    STATS = sb("STATS", [1, 2048], F32)
    _arbn = {"L1": 16896, "L2": 25216, "L3": 25856, "L4": 20480,
             "L5": 20480, "L6": 25856, "L7": 15360}[LM]
    ARB = sb("ARB", [128, _arbn], BF16)
    ARF = sb("ARF", [128, 1792], F32)
    WINF = sb("WINF", [128, 1024], F32) if LM == "L1" else None

    PSA = st.enter_context(nc.psum_tensor("PSA", [128, 2048], F32))
    PSB = st.enter_context(nc.psum_tensor("PSB", [128, 1536], F32))
    PST = st.enter_context(nc.psum_tensor("PST", [128, 1024], BF16))

    ab = Arena(ARB, _arbn)
    af = Arena(ARF, 1792)
    P = Prog(nc, st)

    def dma(dst, src, waits=None, sem=None):
        return P.step("sync", lambda e: e.dma_start(out=dst, in_=src),
                      waits=waits, dmas=1, sem=sem)

    wstate = {"slot": 0, "busy": [None, None], "stbusy": [None, None]}

    def load_wtile(src_oc_ap, I=DM, gn=False, waits=None):
        slot = wstate["slot"]
        wstate["slot"] = 1 - slot
        dst = WBF[:, slot * 4096: slot * 4096 + I]
        evs = list(waits or [])
        last = None
        for hf in range((I + 1023) // 1024):
            n = min(1024, I - hf * 1024)
            stg = WST[:, (hf % 2) * 1024:(hf % 2) * 1024 + n]
            w = list(evs)
            if wstate["stbusy"][hf % 2] is not None:
                w.append(wstate["stbusy"][hf % 2])
            ld = dma(stg, src_oc_ap[:, hf * 1024: hf * 1024 + n], waits=w,
                     sem="s_dw%d" % (hf % 2))
            cw = [ld]
            if wstate["busy"][slot] is not None:
                cw.append(wstate["busy"][slot])
                wstate["busy"][slot] = None
            dp = dst[:, hf * 1024: hf * 1024 + n]
            if gn:
                ev = P.step("gpsimd", lambda e, d=dp, s=stg: e.tensor_scalar(
                    d, s, GNC[:, 0:1], None, ALU.mult), waits=cw)
            else:
                ev = P.step("gpsimd", lambda e, d=dp, s=stg: e.tensor_copy(d, s),
                            waits=cw)
            wstate["stbusy"][hf % 2] = ev
            last = ev
        return dst, slot, last

    def w_done(slot, ev):
        wstate["busy"][slot] = ev

    def load_arena_const(dst, src_ap, ncols, scale=None):
        for hf in range((ncols + 1023) // 1024):
            n = min(1024, ncols - hf * 1024)
            stg = WST[:, (hf % 2) * 1024:(hf % 2) * 1024 + n]
            w = [wstate["stbusy"][hf % 2]] if wstate["stbusy"][hf % 2] else []
            ld = dma(stg, src_ap[:, hf * 1024:hf * 1024 + n], waits=w + [P.event()],
                     sem="s_dw%d" % (hf % 2))
            dp = dst[:, hf * 1024:hf * 1024 + n]
            if scale is None:
                ev = P.step("gpsimd", lambda e, d=dp, s=stg: e.tensor_copy(d, s),
                            waits=[ld])
            else:
                ev = P.step("gpsimd", lambda e, d=dp, s=stg: e.tensor_scalar(
                    d, s, scale, None, ALU.mult), waits=[ld])
            wstate["stbusy"][hf % 2] = ev
        return P.event()

    def mm_chain(psum, wt, rhsfn, nk, wev=None, extra=None):
        waits = [P.event()]
        if wev is not None:
            waits.append(wev)
        if extra:
            waits.extend(extra)

        def fn(e):
            return [e.matmul(psum, wt[:, f * 128:(f + 1) * 128], rhsfn(f),
                             start=(f == 0), stop=(f == nk - 1),
                             skip_group_check=True) for f in range(nk)]
        return P.step("tensor", fn, waits=waits, incs=nk)

    def mm1(psum, lhs, rhs, start=True, stop=True, waits=None):
        return P.step("tensor", lambda e: e.matmul(
            psum, lhs, rhs, start=start, stop=stop, skip_group_check=True),
            waits=waits)

    def transp(psum_out, in_ap, ident, waits=None):
        return P.step("tensor", lambda e: e.transpose(psum_out, in_ap, ident),
                      waits=waits)

    def ln_stats(t0, n):
        ab.push()
        xsq = ab.alloc(n)
        s1 = PSB[0:1, 0:n]
        s2 = PSB[0:1, 512:512 + n]
        ev_sq = None
        for ch in range(32):
            xg = XR[:, ch * TLOC + t0: ch * TLOC + t0 + n]
            P.step("tensor", lambda e, r=xg, c=ch: e.matmul(
                s1, ONESB[:, 0:1], r, start=(c == 0), stop=(c == 31),
                skip_group_check=True))
            w = [P.event()] + ([ev_sq] if ev_sq else [])
            ev_sq = P.step("scalar", lambda e, i=xg: e.activation(
                xsq, i, AF.Square), waits=w)
            P.step("tensor", lambda e, c=ch: e.matmul(
                s2, ONESB[:, 0:1], xsq, start=(c == 0), stop=(c == 31),
                skip_group_check=True))
        ab.pop()
        arow = STATS[0:1, t0:t0 + n]
        brow = STATS[0:1, 1024 + t0:1024 + t0 + n]
        t1 = ROWS[0:1, 256:256 + n]
        t2 = ROWS[0:1, 768:768 + n]
        P.step("vector", lambda e: e.tensor_scalar(t1, s1, 1.0 / DM, None, ALU.mult))
        P.step("vector", lambda e: e.tensor_mul(t2, t1, t1))
        P.step("vector", lambda e: e.scalar_tensor_tensor(
            t2, s2, 1.0 / DM, t2, ALU.mult, ALU.subtract))
        P.step("scalar", lambda e: e.activation(t2, t2, AF.Ln, bias=CKON[0:1, 0:1]))
        P.step("scalar", lambda e: e.activation(arow, t2, AF.Exp, scale=CKON[0:1, 3:4]))
        P.step("vector", lambda e: e.tensor_mul(brow, t1, arow))
        P.step("vector", lambda e: e.tensor_scalar(brow, brow, -1.0, None, ALU.mult))
        return P.event()

    def ln_bcast(t0, n, abuf, fbuf=None):
        arow = STATS[0:1, t0:t0 + n]
        brow = STATS[0:1, 1024 + t0:1024 + t0 + n]
        pa = PSB[:, 0:n]
        pb = PSB[:, 512:512 + n]
        mm1(pa, ONESF[0:1, :], arow)
        mm1(pb, ONESF[0:1, :], brow)
        A = abuf[:, 0:n]
        B = abuf[:, n:2 * n]
        P.step("vector", lambda e: e.tensor_copy(A, pa))
        P.step("vector", lambda e: e.tensor_copy(B, pb))
        if fbuf is not None:
            P.step("vector", lambda e: e.tensor_copy(fbuf[:, 0:n], pa))
            P.step("vector", lambda e: e.tensor_copy(fbuf[:, n:2 * n], pb))
        return A, B, P.event()

    def ln_apply(t0, n, A, B, dstfn):
        for ch in range(32):
            xg = XR[:, ch * TLOC + t0: ch * TLOC + t0 + n]
            d = dstfn(ch)
            P.step("vector", lambda e, x=xg, d2=d: e.tensor_mul(d2, x, A))
            P.step("vector", lambda e, d2=d: e.tensor_add(d2, d2, B))
        return P.event()

    def rope_psum(psum, dst, cs, sn, sc, bias=None):
        if bias is not None:
            P.step("scalar", lambda e: e.activation(dst, psum, AF.Identity, bias=bias))
            src = dst
        else:
            src = psum
        P.step("scalar", lambda e: e.activation(sc[0:64, :], src[64:128, :],
                                                AF.Identity))
        P.step("scalar", lambda e: e.activation(sc[64:128, :], src[0:64, :],
                                                AF.Identity))
        P.step("vector", lambda e: e.tensor_mul(sc, sc, sn))
        P.step("vector", lambda e: e.tensor_mul(dst, src, cs))
        P.step("vector", lambda e: e.tensor_add(dst, dst, sc))
        return P.event()

    kvh_of = lambda h: h // 4
    pg = None

    dma(C_ID[:, :], id_bf[:, :])
    dma(C_IDF[:, :], id_f32[:, :])
    if LM == "L2":
        dma(C_MB[:, :], mbrep[:, :])
        dma(C_GV[:, :], gvec[:, :])
        dma(C_IV[:, :], ivec[:, :])
        dma(GNC[:, 0:1], gnw[:, :])
    if LM == "L5":
        dma(C_DM[:, :], dmask[:, :])
        dma(SMALL[:, 2:3], bv_b[:, :])
        dma(SMALL[:, 3:4], bk_b[:, :])
        for i in range(4):
            dma(SMALL[:, 4 + i:5 + i], bq_b[i * 128:(i + 1) * 128, :])
    if LM in ("L4", "L7"):
        dma(C_REP[:, :], rep_f[:, :])
    if LM == "L1":
        dma(SMALL[0:32, 0:1], dtb[:, :])
        ev0 = dma(SMALL[0:32, 1:2], alog[:, :])
    if LM in ("L3", "L6"):
        dma(ROWS[0:1, 0:32], gb1[:, :])
        dma(ROWS[0:1, 32:40], gb2[:, :])
        dma(ROWS[0:1, 72:104], lng[:, :])
        dma(ROWS[0:1, 104:136], lnb[:, :])
    P.step("gpsimd", lambda e: e.memset(ONESB[:, :], 1.0), waits=[])
    P.step("gpsimd", lambda e: e.memset(ONESF[:, :], 1.0), waits=[])
    P.step("gpsimd", lambda e: e.memset(ONESF40[:, :], P40), waits=[])
    for col, val in ((0, LN_EPS), (1, 1.0), (2, -1.0), (3, -0.5),
                     (4, -SOFT_B), (5, INV_SQRT_D)):
        P.step("gpsimd", lambda e, c2=col, v2=val: e.memset(CKON[:, c2:c2 + 1], v2),
               waits=[])
    if LM == "L1":
        P.step("scalar", lambda e: e.activation(SMALL[0:32, 1:2], SMALL[0:32, 1:2],
                                                AF.Exp), waits=[ev0])
        P.step("vector", lambda e: e.tensor_scalar(SMALL[0:32, 1:2],
                                                   SMALL[0:32, 1:2],
                                                   -1.0, None, ALU.mult))
    if LM in ("L3", "L6"):
        mm1(PSB[:, 0:32], ONESF[0:1, :], ROWS[0:1, 72:104])
        mm1(PSB[:, 32:64], ONESF[0:1, :], ROWS[0:1, 104:136])
        P.step("vector", lambda e: e.tensor_copy(GLN[:, 0:64], PSB[:, 0:64]))
        gw2s = dma(SMALL[0:32, 8:16], gw2[:, :])
        P.step("gpsimd", lambda e: e.tensor_copy(G2B[:, :], SMALL[0:32, 8:16]),
               waits=[gw2s, P.event()])
    if LM in ("L1", "L2"):
        for dst, src in ((C_CS, acs), (C_SN, asn)):
            s0 = dma(WST[:, 0:1024], src[:, :],
                     waits=[wstate["stbusy"][0]] if wstate["stbusy"][0] else [],
                     sem="s_dw0")
            wstate["stbusy"][0] = P.step(
                "gpsimd", lambda e, d=dst: e.tensor_copy(d[:, :], WST[:, 0:1024]),
                waits=[s0])
    if LM != "L5":
        for ch in range(32):
            stg = WST[:, (ch % 2) * 1024:(ch % 2) * 1024 + 1024]
            w = [wstate["stbusy"][ch % 2]] if wstate["stbusy"][ch % 2] else []
            ld = dma(stg, xt[ch * 128:(ch + 1) * 128, :], waits=w,
                     sem="s_dw%d" % (ch % 2))
            wstate["stbusy"][ch % 2] = P.step(
                "gpsimd", lambda e, d=XR[:, ch * TLOC:(ch + 1) * TLOC], s=stg:
                e.tensor_copy(d, s), waits=[ld])
    if LM in ("L1", "L2"):
        P.step("gpsimd", lambda e: e.memset(SLOC[:, :], 0.0))

    def export_x(dst):
        for ch in range(32):
            stg = WST[:, (ch % 2) * 1024:(ch % 2) * 1024 + 1024]
            w = [wstate["stbusy"][ch % 2]] if wstate["stbusy"][ch % 2] else []
            ev = P.step("vector",
                        lambda e, s=stg, x=XR[:, ch * TLOC:(ch + 1) * TLOC]:
                        e.tensor_copy(s, x), waits=w + [P.event()])
            de = dma(dst[ch * 128:(ch + 1) * 128, :], stg, waits=[ev],
                     sem="s_dw%d" % (ch % 2))
            wstate["stbusy"][ch % 2] = de

    if LM == "L1":
            ab.push()
            dma(WINF[:, :], win_a[:, :])
            ZERW = af.alloc(128)
            P.step("vector", lambda e: e.memset(ZERW, 0.0))
    
            NA = 256
            for g in range(TLOC // NA):
                t0 = g * NA
                ab.push(); af.push()
                ln_stats(t0, NA)
                ABT = ab.alloc(2 * NA)
                ABF = af.alloc(2 * NA)
                A, Bv, _ = ln_bcast(t0, NA, ABT, fbuf=ABF)
                XNb = ab.alloc(32 * NA)
                XN = lambda ch, B=XNb: B[:, ch * NA:(ch + 1) * NA]
                ln_apply(t0, NA, A, Bv, XN)
                KF = ab.alloc(8 * NA)
                VF = ab.alloc(8 * NA)
                SC = ab.alloc(NA)
                dt_f = af.alloc(NA)
                scr = af.alloc(NA)
                XF = af.alloc(NA)
                dt_chain(t0, NA, XN, dt_f, scr, XF, ABF[:, 0:NA], ABF[:, NA:2 * NA])
                for oc in range(8):
                    wt, slot, wev = load_wtile(wk_a[oc])
                    mm_chain(PSA[:, 0:NA], wt, XN, 32, wev, extra=[pg] if pg else None)
                    ev = rope_psum(PSA[:, 0:NA], KF[:, oc * NA:(oc + 1) * NA],
                                   C_CS[:, t0:t0 + NA], C_SN[:, t0:t0 + NA], SC)
                    pg = ev
                    w_done(slot, ev)
                for oc in range(8):
                    wt, slot, wev = load_wtile(wv_a[oc])
                    mm_chain(PSA[:, 0:NA], wt, XN, 32, wev, extra=[pg] if pg else None)
                    ev = P.step("vector", lambda e, o=oc, V=VF: e.tensor_copy(
                        V[:, o * NA:(o + 1) * NA], PSA[:, 0:NA]))
                    pg = ev
                    w_done(slot, ev)
                for oc in range(8):
                    dma(ka_d[oc * 128:(oc + 1) * 128, t0:t0 + NA], KF[:, oc * NA:(oc + 1) * NA])
                KTM = ab.alloc(1024)
                VTM = ab.alloc(1024)
                KD = ab.alloc(128)
                for cc in range(NA // 128):
                    c = (t0 + cc * 128) // 128
                    t0c = t0 + cc * 128
                    af.push()
                    DTT = af.alloc(32)
                    transp(PSB[:, 1024:1056], dt_f[0:32, cc * 128:(cc + 1) * 128],
                           C_IDF[0:32, 0:32])
                    P.step("vector", lambda e, T=DTT: e.tensor_copy(T, PSB[:, 1024:1056]))
                    dma(dtt_d[t0c:t0c + 128, :], DTT)
                    parts = chunk_gparts(t0c, DTT)
                    for kv in range(8):
                        transp(PST[:, 0:128], KF[:, kv * NA + cc * 128: kv * NA + cc * 128 + 128], C_ID[:, :])
                        P.step("vector", lambda e, k2=kv, T=KTM: e.tensor_copy(
                            T[:, k2 * 128:(k2 + 1) * 128], PST[:, 0:128]))
                        transp(PST[:, 0:128], VF[:, kv * NA + cc * 128: kv * NA + cc * 128 + 128], C_ID[:, :])
                        P.step("vector", lambda e, k2=kv, T=VTM: e.tensor_copy(
                            T[:, k2 * 128:(k2 + 1) * 128], PST[:, 0:128]))
                    dma(va_d[t0c:t0c + 128, :], VTM)
                    if c == 0:
                        P.step("vector", lambda e: e.tensor_copy(ROWS[0:1, 128:160],
                                                                 ROWS[0:1, 96:128]))
                    else:
                        P.step("vector", lambda e: e.tensor_add(
                            ROWS[0:1, 128:160], ROWS[0:1, 128:160], ROWS[0:1, 96:128]))
                    chunk_states(KTM, VTM, parts["AUX"], KD, False)
                    af.pop()
                ab.pop(); af.pop()
            ab.pop()
            gview = grow_d.rearrange("o (h c t) -> (o h) c t", h=32, c=8, t=128)
            dma(gview, GHM[:, :].rearrange("h (c t) -> h c t", c=8, t=128))
            csview = agg_o[0:4096, :].rearrange("(h d) e -> d h e", h=32, d=128)
            dma(csview, SLOC[:, :].rearrange("d (h e) -> d h e", h=32, e=128))
            P.step("scalar", lambda e: e.activation(ROWS[0:1, 160:192], ROWS[0:1, 128:160],
                                                    AF.Exp))
            dma(agg_o[4096:4097, 0:32], ROWS[0:1, 160:192])
            dma(ghm_o[:, :], GHM[:, :])
            dma(stats_o[:, :], STATS[:, :])
    if LM == "L2":
            dma(GHM[:, :], ghm_i[:, :])
            dma(STATS[:, :], stats_i[:, :])
            P.step("gpsimd", lambda e: e.memset(SLOC[:, :], 0.0))
            af.push()
            CSB = af.alloc(256)
            CDB = af.alloc(32)
            csbusy = [None, None]
            for m in range(NCORE):
                base = m * 4104
                ld = dma(ROWS[0:1, 160:192], agg_all[base + 4096:base + 4097, 0:32],
                         waits=[P.event()])
                mm1(PSB[:, 1024:1056], ONESF[0:1, :], ROWS[0:1, 160:192], waits=[ld, P.event()])
                P.step("vector", lambda e, m2=m: e.tensor_scalar(
                    CDB, PSB[:, 1024:1056], C_GV[:, m2:m2 + 1], None, ALU.mult))
                P.step("vector", lambda e, m2=m: e.tensor_scalar(
                    CDB, CDB, C_IV[:, m2:m2 + 1], None, ALU.add))
                cview = agg_all[base:base + 4096, :].rearrange("(h d) e -> d h e", h=32, d=128)
                for h in range(32):
                    sl = 128 * (h % 2)
                    w = [csbusy[h % 2]] if csbusy[h % 2] else []
                    if h < 2:
                        w = w + [P.event()]
                    ld2 = dma(CSB[:, sl:sl + 128], cview[:, h, :], waits=w,
                      sem="s_dc%d" % (h % 2))
                    ssl = SLOC[:, h * 128:(h + 1) * 128]
                    P.step("vector", lambda e, s=ssl, h2=h: e.tensor_scalar(
                        s, s, CDB[:, h2:h2 + 1], None, ALU.mult), waits=[P.event(), ld2])
                    ev = P.step("vector", lambda e, s=ssl, m2=m, sl2=sl:
                                e.scalar_tensor_tensor(s, CSB[:, sl2:sl2 + 128],
                                                       C_GV[0:128, m2:m2 + 1], s,
                                                       ALU.mult, ALU.add))
                    csbusy[h % 2] = ev
            P.step("vector", lambda e: e.tensor_copy(PREVB[:, :], SLOC[:, :]))
            af.pop()
            grbusy = [None]
            for c in range(8):
                t0c = c * 128
                ab.push(); af.push()
                ABT = ab.alloc(256)
                A, Bv, _ = ln_bcast(t0c, 128, ABT)
                XNb = ab.alloc(4096)
                XN = lambda ch, B=XNb: B[:, ch * 128:(ch + 1) * 128]
                ln_apply(t0c, 128, A, Bv, XN)
                QF = ab.alloc(4096)
                GATE = ab.alloc(4096)
                SC = ab.alloc(128)
                EH = ab.alloc(1024)
                YTM = ab.alloc(4096)
                Y3 = ab.alloc(4096)
                KFc = ab.alloc(1024)
                VTMc = ab.alloc(1024)
                KTMB = ab.alloc(1024)
                KD = ab.alloc(128)
                for kv in range(8):
                    dma(KFc[:, kv * 128:(kv + 1) * 128],
                        ka_d[kv * 128:(kv + 1) * 128, t0c:t0c + 128])
                dma(VTMc[:, :], va_d[t0c:t0c + 128, :])
                DTT = af.alloc(32)
                dtld = dma(DTT, dtt_d[t0c:t0c + 128, :])
                T1 = af.alloc(1024)
                YT1 = af.alloc(128)
                cs = C_CS[:, t0c:t0c + 128]
                sn = C_SN[:, t0c:t0c + 128]
                for oc in range(32):
                    wt, slot, wev = load_wtile(wq_a[oc])
                    mm_chain(PSA[:, 0:128], wt, XN, 32, wev, extra=[pg] if pg else None)
                    ev = rope_psum(PSA[:, 0:128], QF[:, oc * 128:(oc + 1) * 128], cs, sn, SC)
                    pg = ev
                    w_done(slot, ev)
                for oc in range(32):
                    wt, slot, wev = load_wtile(wg_a[oc])
                    mm_chain(PSA[:, 0:128], wt, XN, 32, wev, extra=[pg] if pg else None)
                    ev = P.step("scalar", lambda e, o=oc, G2=GATE: e.activation(
                        G2[:, o * 128:(o + 1) * 128], PSA[:, 0:128], AF.Silu))
                    pg = ev
                    w_done(slot, ev)
                for kv in range(8):
                    transp(PST[:, 0:128], KFc[:, kv * 128:(kv + 1) * 128], C_ID[:, :])
                    P.step("vector", lambda e, k2=kv, T=KTMB: e.tensor_copy(
                        T[:, k2 * 128:(k2 + 1) * 128], PST[:, 0:128]))
                P._add_wait("vector", dtld)
                parts = chunk_gparts(t0c, DTT, need_bias=True)
                AUX = parts["AUX"]; BIA = parts["BIA"]; EXG = parts["EXG"]
                for q8 in range(4):
                    h0 = q8 * 8
                    gr = GRW[0:1, 0:1024]
                    w = [grbusy[0]] if grbusy[0] else []
                    gsrc = grow_d.rearrange("o (h c t) -> o h (c t)", h=32, c=8, t=128)[
                        0:1, h0:h0 + 8, c * 128:(c + 1) * 128]
                    gld = dma(gr.rearrange("p (h t) -> p h t", h=8, t=128), gsrc,
                              waits=w + [P.event()])
    
                    def bc_fn(e, g2=gr):
                        out = []
                        for i in range(8):
                            out.append(e.matmul(PSA[:, i * 128:(i + 1) * 128], ONESF[0:1, :],
                                           g2[0:1, i * 128:(i + 1) * 128], start=True,
                                           stop=True, skip_group_check=True))
                        return out
                    P.step("tensor", bc_fn, waits=[P.event(), gld], incs=8)
                    P.step("vector", lambda e, T=T1: e.tensor_add(T, PSA[:, 0:1024], C_MB[:, :]))
                    grbusy[0] = P.event()
                    for hh in range(8):
                        h = h0 + hh
                        P.step("scalar", lambda e, i=hh, h2=h, E2=EH, T=T1, B2=BIA:
                               e.activation(E2[:, i * 128:(i + 1) * 128],
                                            T[:, i * 128:(i + 1) * 128],
                                            AF.Exp, bias=B2[:, h2:h2 + 1]))
    
                    def sc_fn(e, h0=h0, K2=KFc, Q2=QF):
                        out = []
                        for i in range(8):
                            h = h0 + i
                            out.append(e.matmul(PSA[:, 1024 + i * 128:1024 + (i + 1) * 128],
                                           K2[:, kvh_of(h) * 128:(kvh_of(h) + 1) * 128],
                                           Q2[:, h * 128:(h + 1) * 128], start=True,
                                           stop=True, skip_group_check=True))
                        return out
                    P.step("tensor", sc_fn, incs=8)
                    P.step("vector", lambda e, E2=EH: e.tensor_mul(E2, E2, PSA[:, 1024:2048]))
                    for hh in range(8):
                        h = h0 + hh
                        kv = kvh_of(h)
                        mm1(PSB[:, 0:128], EH[:, hh * 128:(hh + 1) * 128],
                            VTMc[:, kv * 128:(kv + 1) * 128])
                        mm1(PSB[:, 128:256], QF[:, h * 128:(h + 1) * 128],
                            PREVB[:, h * 128:(h + 1) * 128])
                        P.step("scalar", lambda e, h2=h, Y2=YT1, X=EXG: e.activation(
                            Y2, PSB[:, 128:256], AF.Identity, scale=X[:, h2:h2 + 1]))
                        P.step("vector", lambda e, h2=h, Y2=YTM, Y1=YT1: e.tensor_add(
                            Y2[:, h2 * 128:(h2 + 1) * 128], PSB[:, 0:128], Y1))
                        P.step("vector", lambda e, h2=h, k2=kv, D2=KD, T=KTMB, X=AUX:
                               e.tensor_scalar(D2, T[:, k2 * 128:(k2 + 1) * 128],
                                               X[:, h2:h2 + 1], None, ALU.mult))
                        mm1(PSB[:, 256:384], KD, VTMc[:, kv * 128:(kv + 1) * 128])
                        ssl = SLOC[:, h * 128:(h + 1) * 128]
                        P.step("vector", lambda e, s=ssl, h2=h, X=AUX: e.tensor_scalar(
                            s, s, X[:, 32 + h2:33 + h2], None, ALU.mult))
                        P.step("vector", lambda e, s=ssl: e.tensor_add(s, s, PSB[:, 256:384]))
                        P.step("vector", lambda e, s=ssl, h2=h: e.tensor_copy(
                            PREVB[:, h2 * 128:(h2 + 1) * 128], s))
                MS = af.alloc(32)
                RST = af.alloc(32)
                P.step("scalar", lambda e, S2=XNb, Y2=YTM: e.activation(S2, Y2, AF.Square))
                P.step("vector", lambda e, M2=MS, S2=XNb: e.tensor_reduce(
                    M2, S2.rearrange("p (h e) -> p h e", h=32, e=128), AX.X, ALU.add))
                P.step("vector", lambda e, M2=MS: e.tensor_scalar(M2, M2, 1.0 / 128.0, None, ALU.mult))
                P.step("scalar", lambda e, M2=MS: e.activation(M2, M2, AF.Ln, bias=CKON[:, 0:1]))
                P.step("scalar", lambda e, R2=RST, M2=MS: e.activation(R2, M2, AF.Exp, scale=CKON[:, 3:4]))
                for h in range(32):
                    P.step("vector", lambda e, h2=h, Y2=YTM, R2=RST: e.tensor_scalar(
                        Y2[:, h2 * 128:(h2 + 1) * 128], Y2[:, h2 * 128:(h2 + 1) * 128],
                        R2[:, h2:h2 + 1], None, ALU.mult))
                for h in range(32):
                    transp(PST[:, 0:128], YTM[:, h * 128:(h + 1) * 128], C_ID[:, :])
                    P.step("vector", lambda e, h2=h, Y2=Y3, G2=GATE: e.tensor_mul(
                        Y2[:, h2 * 128:(h2 + 1) * 128], PST[:, 0:128],
                        G2[:, h2 * 128:(h2 + 1) * 128]))
                Y3f = lambda f, B=Y3: B[:, f * 128:(f + 1) * 128]
                for oc in range(32):
                    wt, slot, wev = load_wtile(wo_a[oc], gn=True)
                    mm_chain(PSA[:, 0:128], wt, Y3f, 32, wev, extra=[pg] if pg else None)
                    xs = XR[:, oc * TLOC + t0c: oc * TLOC + t0c + 128]
                    ev = P.step("vector", lambda e, x=xs: e.tensor_add(x, x, PSA[:, 0:128]))
                    pg = ev
                    w_done(slot, ev)
                ab.pop(); af.pop()
            export_x(x_o)

    def ffn_phase(r, part):
        pg = None
        ab.push(); af.push()
        if part == "b":
            EQ1A = ab.alloc(64); EQ2A = ab.alloc(64)
            TMP = ab.alloc(1024)
            TS8 = ab.alloc(32); ALB = ab.alloc(16); WFM = ab.alloc(1024)
            W1A = af.alloc(8); W2A = af.alloc(8)
            CN = af.alloc(128)
            dma(EQ1A, eq_i[:, 0:64]); dma(EQ2A, eq_i[:, 64:128])
            dma(TMP, tmp_i[:, :])
            dma(W1A, w12_i[:, 0:8]); dma(W2A, w12_i[:, 8:16])
            ld = dma(CN[0:1, :], cnt_all.rearrange("(o r) c -> o (r c)", o=1, r=8))
            P.step("vector", lambda e: e.tensor_reduce(
                ROWS[0:1, 40:56], CN[0:1, :].rearrange("p (r c) -> p c r", r=8, c=16),
                AX.X, ALU.add), waits=[ld, P.event()])
            ffn_part_b(r, EQ1A, EQ2A, TMP, TS8, ALB, WFM, W1A, W2A)
            ab.pop(); af.pop()
            return
        GW1B = ab.alloc(1024)
        load_arena_const(GW1B, gw1, 1024)
        ASTB = ab.alloc(4096)
        load_arena_const(ASTB, ast_f, 4096)
        EQ1A = ab.alloc(64)
        EQ2A = ab.alloc(64)
        WFM = ab.alloc(1024)
        TMP = ab.alloc(1024)
        HFM = ab.alloc(1024)
        TS8 = ab.alloc(32)
        HRLB = ab.alloc(32)
        ALB = ab.alloc(16)
        W1A = af.alloc(8)
        W2A = af.alloc(8)
        SM1 = af.alloc(16)
        G2M = af.alloc(8)
        HMF = af.alloc(32)
        SQF = af.alloc(32)
        for g in range(2):
            t0 = g * 512
            ab.push()
            ln_stats(t0, 512)
            ABT = ab.alloc(1024)
            A, Bv, _ = ln_bcast(t0, 512, ABT)
            XNb = ab.alloc(32 * 512)
            XN = lambda ch, B=XNb: B[:, ch * 512:(ch + 1) * 512]
            ln_apply(t0, 512, A, Bv, XN)
            for tc in range(4):
                tg = g * 4 + tc
                hps = PSB[:, 0:32]

                def gfn(e, tc2=tc, B=XNb):
                    out = []
                    for f in range(32):
                        out.append(e.matmul(hps, B[:, f * 512 + tc2 * 128:
                                              f * 512 + tc2 * 128 + 128],
                                       GW1B[:, f * 32:(f + 1) * 32],
                                       start=(f == 0), stop=False,
                                       skip_group_check=True))
                    return out
                P.step("tensor", gfn, incs=32)
                mm1(hps, ONESF[0:1, :], ROWS[0:1, 0:32], start=False, stop=True)
                P.step("vector", lambda e, hp=hps: e.tensor_reduce(SM1[:, 0:1], hp, AX.X,
                                                           ALU.add))
                P.step("vector", lambda e: e.tensor_scalar(
                    SM1[:, 0:1], SM1[:, 0:1], 1.0 / 32.0, None, ALU.mult))
                P.step("vector", lambda e, hp=hps: e.tensor_scalar(
                    HMF, hp, SM1[:, 0:1], None, ALU.subtract))
                P.step("scalar", lambda e: e.activation(SQF, HMF, AF.Square))
                P.step("vector", lambda e: e.tensor_reduce(SM1[:, 1:2], SQF, AX.X,
                                                           ALU.add))
                P.step("vector", lambda e: e.tensor_scalar(
                    SM1[:, 1:2], SM1[:, 1:2], 1.0 / 32.0, None, ALU.mult))
                P.step("scalar", lambda e: e.activation(SM1[:, 1:2], SM1[:, 1:2], AF.Ln, bias=CKON[:, 0:1]))
                P.step("scalar", lambda e: e.activation(SM1[:, 1:2], SM1[:, 1:2], AF.Exp, scale=CKON[:, 3:4]))
                P.step("vector", lambda e: e.tensor_scalar(
                    HMF, HMF, SM1[:, 1:2], None, ALU.mult))
                P.step("vector", lambda e: e.tensor_mul(HMF, HMF, GLN[:, 0:32]))
                P.step("vector", lambda e: e.tensor_add(HMF, HMF, GLN[:, 32:64]))
                P.step("scalar", lambda e: e.activation(HRLB, HMF, AF.Relu))
                transp(PST[0:32, 0:128], HRLB, C_ID[:, :])
                P.step("vector", lambda e, tc2=tc: e.tensor_copy(
                    HFM[0:32, tc2 * 128:(tc2 + 1) * 128], PST[0:32, 0:128]))
                g2ps = PSB[:, 64:72]
                mm1(g2ps, HFM[0:32, tc * 128:(tc + 1) * 128], G2B[:, :])
                mm1(g2ps, ONESF[0:1, :], ROWS[0:1, 32:40], start=False, stop=True)
                P.step("vector", lambda e, gp=g2ps: e.tensor_reduce(SM1[:, 2:3], gp, AX.X,
                                                           ALU.max))
                eq1 = EQ1A[:, tg * 8:(tg + 1) * 8]
                eq2 = EQ2A[:, tg * 8:(tg + 1) * 8]
                P.step("vector", lambda e, q=eq1, gp=g2ps: e.tensor_scalar(
                    q, gp, SM1[:, 2:3], None, ALU.is_equal))
                P.step("vector", lambda e, q=eq1, gp=g2ps: e.scalar_tensor_tensor(
                    G2M, q, -1e30, gp, ALU.mult, ALU.add))
                P.step("vector", lambda e: e.tensor_reduce(SM1[:, 3:4], G2M, AX.X,
                                                           ALU.max))
                P.step("vector", lambda e, q=eq2: e.tensor_scalar(
                    q, G2M, SM1[:, 3:4], None, ALU.is_equal))
                P.step("vector", lambda e: e.tensor_sub(SM1[:, 4:5], SM1[:, 2:3],
                                                        SM1[:, 3:4]))
                P.step("scalar", lambda e, tg2=tg: e.activation(
                    W1A[:, tg2:tg2 + 1], SM1[:, 4:5], AF.Sigmoid))
                P.step("vector", lambda e, tg2=tg: e.tensor_scalar(
                    W2A[:, tg2:tg2 + 1], W1A[:, tg2:tg2 + 1], -1.0, 1.0,
                    ALU.mult, ALU.add))
                mm1(PSB[0:8, 1024:1025], eq1, ONESB[:, 0:1],
                    start=(tg == 0), stop=(tg == 7))
                mm1(PSB[0:8, 1025:1026], eq2, ONESB[:, 0:1],
                    start=(tg == 0), stop=(tg == 7))
            mm_chain(PSA[:, 1024:1536], ASTB, XN, 32)
            P.step("vector", lambda e, g2=g: e.tensor_copy(
                TMP[:, g2 * 512:(g2 + 1) * 512], PSA[:, 1024:1536]))
            for oc in range(32):
                wt, slot, wev = load_wtile(wm_f[oc])
                mm_chain(PSA[:, 0:512], wt, XN, 32, wev, extra=[pg] if pg else None)
                xs = XR[:, oc * TLOC + t0: oc * TLOC + t0 + 512]
                ev = P.step("vector", lambda e, x=xs: e.tensor_add(x, x, PSA[:, 0:512]))
                pg = ev
                w_done(slot, ev)
            ab.pop()
        CNTF = af.alloc(2)
        P.step("vector", lambda e: e.tensor_copy(CNTF[0:8, :], PSB[0:8, 1024:1026]))
        cview = cnt_o.rearrange("o (e k) -> (o e) k", e=8, k=2)
        dma(cview, CNTF[0:8, :], waits=[P.event()])
        dma(tmp_o[:, :], TMP)
        dma(eq_o[:, 0:64], EQ1A); dma(eq_o[:, 64:128], EQ2A)
        dma(w12_o[:, 0:8], W1A); dma(w12_o[:, 8:16], W2A)
        export_x(x_o)
        ab.pop(); af.pop()
        return

    def ffn_part_b(r, EQ1A, EQ2A, TMP, TS8, ALB, WFM, W1A, W2A):
        alv = ROWS[0:1, 56:72].rearrange("p (e k) -> p k e", e=8, k=2)
        P.step("vector", lambda e: e.tensor_scalar(
            ROWS[0:1, 56:72], ROWS[0:1, 40:56], CAP, None, ALU.is_le))
        P.step("vector", lambda e: e.tensor_copy(ROWS[0:1, 192:200], alv[0:1, 0, :]))
        P.step("vector", lambda e: e.tensor_copy(ROWS[0:1, 200:208], alv[0:1, 1, :]))
        mm1(PSB[:, 64:72], ONESF[0:1, :], ROWS[0:1, 192:200])
        mm1(PSB[:, 72:80], ONESF[0:1, :], ROWS[0:1, 200:208])
        P.step("vector", lambda e: e.tensor_copy(ALB[:, 0:16], PSB[:, 64:80]))
        for tg in range(8):
            eq1 = EQ1A[:, tg * 8:(tg + 1) * 8]
            eq2 = EQ2A[:, tg * 8:(tg + 1) * 8]
            P.step("vector", lambda e, q=eq1, tg2=tg: e.tensor_scalar(
                TS8[:, 0:8], q, W1A[:, tg2:tg2 + 1], None, ALU.mult))
            P.step("vector", lambda e: e.tensor_mul(TS8[:, 0:8], TS8[:, 0:8],
                                                    ALB[:, 0:8]))
            P.step("vector", lambda e, q=eq2, tg2=tg: e.tensor_scalar(
                TS8[:, 8:16], q, W2A[:, tg2:tg2 + 1], None, ALU.mult))
            P.step("vector", lambda e: e.tensor_mul(TS8[:, 8:16], TS8[:, 8:16],
                                                    ALB[:, 8:16]))
            P.step("vector", lambda e: e.tensor_add(TS8[:, 0:8], TS8[:, 0:8],
                                                    TS8[:, 8:16]))
            transp(PST[0:8, 0:128], TS8[:, 0:8], C_ID[:, :])
            P.step("vector", lambda e, tg2=tg: e.tensor_copy(
                WFM[0:8, tg2 * 128:(tg2 + 1) * 128], PST[0:8, 0:128]))
        BSB = ab.alloc(4096)
        load_arena_const(BSB, bst_f, 4096, scale=1.0 / 16.0)
        WRP = ab.alloc(512)
        TMPW = ab.alloc(512)
        OF = af.alloc(512)
        for g in range(2):
            t0 = g * 512
            mm1(PSA[:, 1536:2048], C_REP[:, :], WFM[0:8, g * 512:(g + 1) * 512])
            P.step("vector", lambda e: e.tensor_copy(WRP, PSA[:, 1536:2048]))
            P.step("vector", lambda e, g2=g: e.tensor_mul(
                TMPW, TMP[:, g2 * 512:(g2 + 1) * 512], WRP))
            for oc in range(32):
                mm1(PSA[:, 0:512], BSB[:, oc * 128:(oc + 1) * 128], TMPW)
                xs = XR[:, oc * TLOC + t0: oc * TLOC + t0 + 512]
                if r == 0:
                    P.step("vector", lambda e, x=xs: e.tensor_add(x, x, PSA[:, 0:512]))
                else:
                    P.step("vector", lambda e, x=xs: e.tensor_add(OF, x, PSA[:, 0:512]))
                    dma(out_d[oc * 128:(oc + 1) * 128, t0:t0 + 512], OF,
                        waits=[P.event()])

    if LM == "L3":
        ffn_phase(0, "a")
    if LM == "L4":
        ffn_phase(0, "b")
        export_x(x_o)

    if LM == "L4":
        ab.push(); af.push()
        for g in range(2):
            t0 = g * 512
            ab.push()
            ln_stats(t0, 512)
            ABT = ab.alloc(1024)
            A, Bv, _ = ln_bcast(t0, 512, ABT)
            XNb = ab.alloc(32 * 512)
            XN = lambda ch, B=XNb: B[:, ch * 512:(ch + 1) * 512]
            ln_apply(t0, 512, A, Bv, XN)
            for ch in range(32):
                dma(xn_o[ch * 128:(ch + 1) * 128, t0:t0 + 512], XN(ch),
                    waits=[P.event()])
            ab.pop()
        ab.pop(); af.pop()
    if LM == "L5":
        ab.push(); af.push()
        ab.push()
        XNG = ab.alloc(16384)
        CSG = ab.alloc(512)
        SNG = ab.alloc(512)
        PREB = ab.alloc(512)
        KRP = ab.alloc(512)
        SCB = ab.alloc(512)
        for tg in range(16):
            t0g = tg * 512
            kc = tg // 2
            lc = (tg % 2) * 512
            lds = []
            for ch in range(32):
                lds.append(dma(XNG[:, ch * 512:(ch + 1) * 512],
                               xn_all[kc * DM + ch * 128: kc * DM + (ch + 1) * 128,
                                        lc:lc + 512],
                               waits=[P.event()] if ch == 0 else []))
            lds.append(dma(CSG, bcs[:, t0g:t0g + 512]))
            lds.append(dma(SNG, bsn[:, t0g:t0g + 512]))
            XNf = lambda f, B=XNG: B[:, f * 512:(f + 1) * 512]
            wt, slot, wev = load_wtile(wk_b[0])
            mm_chain(PSA[:, 0:512], wt, XNf, 32, wev, extra=lds + ([pg] if pg else []))
            rope_psum(PSA[:, 0:512], KRP, CSG, SNG, SCB, bias=SMALL[:, 3:4])
            ev = dma(kb_d[:, t0g:t0g + 512], KRP, waits=[P.event()])
            pg = P.event()
            w_done(slot, P.event())
            wt, slot, wev = load_wtile(wv_b[0])
            mm_chain(PSA[:, 0:512], wt, XNf, 32, wev, extra=[pg] if pg else None)
            P.step("scalar", lambda e: e.activation(PREB, PSA[:, 0:512], AF.Identity,
                                                    bias=SMALL[:, 2:3]))
            pg = P.event()
            w_done(slot, P.event())
            for tt in range(4):
                transp(PST[:, 0:128], PREB[:, tt * 128:(tt + 1) * 128], C_ID[:, :])
                P.step("vector", lambda e: e.tensor_copy(SCB[:, 0:128], PST[:, 0:128]))
                dma(vb_d[t0g + tt * 128: t0g + (tt + 1) * 128, :], SCB[:, 0:128],
                    waits=[P.event()])
            for hq in range(4):
                wt, slot, wev = load_wtile(wq_b[hq])
                mm_chain(PSA[:, 0:512], wt, XNf, 32, wev, extra=[pg] if pg else None)
                rope_psum(PSA[:, 0:512], KRP, CSG, SNG, SCB,
                          bias=SMALL[:, 4 + hq:5 + hq])
                dma(qb_d[hq * 128:(hq + 1) * 128, t0g:t0g + 512], KRP,
                    waits=[P.event()])
                pg = P.event()
                w_done(slot, P.event())

        ab.pop()
        ab.push()
        QH = ab.alloc(512)
        KW = ab.alloc(4096)
        VW = ab.alloc(4096)
        PT = ab.alloc(512)
        CXN = ab.alloc(512)
        LBB = af.alloc(512)
        kwbusy = [None, None]
        qbusy = None
        for hq in range(4):
            for qg in range(16):
                nkv = 4 * (qg + 1)
                qld = dma(QH, qb_d[hq * 128:(hq + 1) * 128, qg * 512:(qg + 1) * 512],
                          waits=[qbusy] if qbusy else [], sem="s_dq")
                wlds = {}
                for j in range(nkv):
                    wslot = (j // 16) % 2
                    if j % 16 == 0:
                        n_t = min(16, nkv - j)
                        w = [kwbusy[wslot]] if kwbusy[wslot] else []
                        l1 = dma(KW[:, wslot * 2048: wslot * 2048 + n_t * 128],
                                 kb_d[:, j * 128: j * 128 + n_t * 128], waits=w,
                                 sem="s_dk")
                        l2 = dma(VW[:, wslot * 2048: wslot * 2048 + n_t * 128]
                                 .rearrange("p (w e) -> p w e", e=128),
                                 vb_d[j * 128: j * 128 + n_t * 128, :].rearrange(
                                     "(w p) e -> p w e", p=128),
                                 waits=w, sem="s_dv")
                        wlds[wslot] = (l1, l2)
                    ko = wslot * 2048 + (j % 16) * 128
                    mm1(PSB[:, 0:512], KW[:, ko:ko + 128], QH,
                        waits=[P.event(), wlds[wslot][0], wlds[wslot][1], qld])
                    P.step("scalar", lambda e: e.activation(
                        PT, PSB[:, 0:512], AF.Exp, bias=CKON[:, 4:5], scale=CKON[:, 5:6]))
                    if j >= nkv - 4:
                        dd = j - (nkv - 4)
                        P.step("vector", lambda e, d2=dd: e.tensor_mul(
                            PT, PT, C_DM[:, d2 * 512:(d2 + 1) * 512]))
                    mm1(PSA[:, 0:512], VW[:, ko:ko + 128], PT,
                        start=(j == 0), stop=(j == nkv - 1))
                    mm1(PSA[0:1, 512:1024], ONESB[:, 0:1], PT,
                        start=(j == 0), stop=(j == nkv - 1))
                    if j % 16 == 15 or j == nkv - 1:
                        kwbusy[wslot] = P.event()
                LS = ROWS[0:1, 256:768]
                P.step("vector", lambda e: e.tensor_scalar(
                    LS, PSA[0:1, 512:1024], P40, None, ALU.mult))
                P.step("vector", lambda e: e.reciprocal(LS, LS))
                mm1(PSA[:, 1024:1536], ONESF40[0:1, :], LS)
                P.step("vector", lambda e: e.tensor_copy(LBB, PSA[:, 1024:1536]))
                P.step("vector", lambda e: e.tensor_mul(CXN, PSA[:, 0:512], LBB))
                qbusy = dma(ctx_o[(qg // 2) * 512 + hq * 128:
                                   (qg // 2) * 512 + (hq + 1) * 128,
                                   (qg % 2) * 512:(qg % 2) * 512 + 512],
                            CXN, waits=[P.event()])
        ab.pop()
        ab.pop(); af.pop()
    if LM == "L6":
        ab.push()
        CTXG = ab.alloc(16384)
        for g in range(2):
            t0 = g * 512
            for ch in range(32):
                dma(CTXG[:, ch * 512:(ch + 1) * 512],
                    ctx_all[ch * 128:(ch + 1) * 128, t0:t0 + 512],
                    waits=[P.event()] if ch == 0 else [])
            ldend = P.event()
            CTf = lambda f, B=CTXG: B[:, f * 512:(f + 1) * 512]
            for oc in range(32):
                wt, slot, wev = load_wtile(wo_b[oc])
                bld = dma(SMALL[:, 8:9], bo_b[oc * 128:(oc + 1) * 128, :],
                          waits=[P.event()])
                mm_chain(PSA[:, 0:512], wt, CTf, 32, wev, extra=[ldend] + ([pg] if pg else []))
                xs = XR[:, oc * TLOC + t0: oc * TLOC + t0 + 512]
                ev = P.step("vector", lambda e, x=xs: e.scalar_tensor_tensor(
                    x, PSA[:, 0:512], SMALL[:, 8:9], x, ALU.add, ALU.add),
                    waits=[P.event(), bld])
                pg = ev
                w_done(slot, ev)
        ab.pop()
        ffn_phase(1, "a")
    if LM == "L7":
        ffn_phase(1, "b")
    P.finish()

    with nc.Block() as block:
        P.build(block)
    return nc, st


_CACHE = {}


def _prog(launch):
    if launch not in _CACHE:
        _CACHE[launch] = build_program(launch)
    return _CACHE[launch][0]


def kernel(**inputs):
    f32 = np.float32
    x = np.asarray(inputs["L0_attn_a_hidden_states"], f32)[0]
    xT = np.ascontiguousarray(x.T)
    cosf, sinf = _rope_tables()
    cos_bf = np.ascontiguousarray(cosf.astype(ml_dtypes.bfloat16))
    sin_bf = np.ascontiguousarray(sinf.astype(ml_dtypes.bfloat16))

    wq_a = _pack_w(np.asarray(inputs["L0_attn_a_Wq"], f32).T)
    wk_a = _pack_w(np.asarray(inputs["L0_attn_a_Wk"], f32).T)
    wv_a = _pack_w(np.asarray(inputs["L0_attn_a_Wv"], f32).T)
    wg_a = _pack_w(np.asarray(inputs["L0_attn_a_Wg"], f32).T)
    wo_a = _pack_w(np.asarray(inputs["L0_attn_a_Wo"], f32).T)
    win_a = _pack_small(np.asarray(inputs["L0_attn_a_Win"], f32).T)
    wq_b = _pack_w(np.asarray(inputs["L0_attn_b_Wq"], f32))
    wk_b = _pack_w(np.asarray(inputs["L0_attn_b_Wk"], f32))
    wv_b = _pack_w(np.asarray(inputs["L0_attn_b_Wv"], f32))
    wo_b = _pack_w(np.asarray(inputs["L0_attn_b_Wo"], f32))
    wm_f = _pack_w(np.asarray(inputs["L0_ffn_weight"], f32).T)
    la = np.asarray(inputs["L0_ffn_lora_A"], f32)
    ast_f = _pack_small(np.ascontiguousarray(la.transpose(1, 0, 2).reshape(DM, 128)))
    bst_f = np.ascontiguousarray(np.asarray(inputs["L0_ffn_lora_B"], f32).reshape(128, DM))
    gw1 = _pack_small(np.asarray(inputs["L0_ffn_gw1"], f32).T)
    gw2 = np.ascontiguousarray(np.asarray(inputs["L0_ffn_gw2"], f32).T)

    mb = np.zeros((128, 128), f32)
    jj, ii = np.meshgrid(np.arange(128), np.arange(128), indexing="ij")
    mb[ii < jj] = -1e30
    mbrep = np.ascontiguousarray(np.tile(mb, (1, 8))).astype(ml_dtypes.bfloat16)
    dm = np.zeros((128, 2048), ml_dtypes.bfloat16)
    for dd in range(4):
        kvi, qi = np.meshgrid(np.arange(128), np.arange(512), indexing="ij")
        dm[:, dd * 512:(dd + 1) * 512] = (qi >= (dd * 128 + kvi)).astype(
            ml_dtypes.bfloat16)
    ident = np.eye(128, dtype=f32)
    idb = np.ascontiguousarray(ident.astype(ml_dtypes.bfloat16))
    rep = np.zeros((8, 128), ml_dtypes.bfloat16)
    for e in range(8):
        rep[e, e * 16:(e + 1) * 16] = 1.0

    bq = np.asarray(inputs["L0_attn_b_bq"], f32)
    bk = np.asarray(inputs["L0_attn_b_bk"], f32)
    bv = np.asarray(inputs["L0_attn_b_bv"], f32)

    com = [{"id_bf": idb, "id_f32": ident} for _ in range(NCORE)]
    xcur = [np.ascontiguousarray(xT[:, c * TLOC:(c + 1) * TLOC]) for c in range(NCORE)]

    def run(launch, extra):
        nc = _prog(launch)
        im = [dict(com[c], xt=xcur[c], **extra[c]) for c in range(NCORE)]
        return run_bass_kernel_spmd(nc, im, core_ids=list(range(NCORE)))

    ex = [{"wk_a": wk_a, "wv_a": wv_a, "win_a": win_a,
           "dtb": np.asarray(inputs["L0_attn_a_dt_bias"], f32).reshape(32, 1),
           "alog": np.asarray(inputs["L0_attn_a_A_log_bias"], f32).reshape(32, 1),
           "acs": np.ascontiguousarray(cosf[:, c * TLOC:(c + 1) * TLOC]),
           "asn": np.ascontiguousarray(sinf[:, c * TLOC:(c + 1) * TLOC])}
          for c in range(NCORE)]
    r1 = run("L1", ex)
    agg_all = np.ascontiguousarray(np.concatenate(
        [np.asarray(r1.results[c]["agg_o"]) for c in range(NCORE)], axis=0))

    ex = []
    for c in range(NCORE):
        gv = np.zeros((128, 8), f32)
        gv[:, :c] = 1.0
        rc = r1.results[c]
        ex.append({"wq_a": wq_a, "wg_a": wg_a, "wo_a": wo_a,
                   "gnw": np.asarray(inputs["L0_attn_a_gnorm_w"], f32).reshape(128, 1),
                   "acs": np.ascontiguousarray(cosf[:, c * TLOC:(c + 1) * TLOC]),
                   "asn": np.ascontiguousarray(sinf[:, c * TLOC:(c + 1) * TLOC]),
                   "mbrep": mbrep, "gvec": gv,
                   "ivec": np.ascontiguousarray(1.0 - gv),
                   "ka_d": rc["ka_d"], "va_d": rc["va_d"], "dtt_d": rc["dtt_d"],
                   "grow_d": rc["grow_d"], "agg_all": agg_all,
                   "ghm_i": rc["ghm_o"], "stats_i": rc["stats_o"]})
    r2 = run("L2", ex)
    xcur = [np.ascontiguousarray(r2.results[c]["x_o"]) for c in range(NCORE)]

    ffnw = {"wm_f": wm_f, "ast_f": ast_f, "gw1": gw1,
            "gb1": np.asarray(inputs["L0_ffn_gb1"], f32).reshape(1, 32),
            "gb2": np.asarray(inputs["L0_ffn_gb2"], f32).reshape(1, 8),
            "lng": np.asarray(inputs["L0_ffn_ln_gamma"], f32).reshape(1, 32),
            "lnb": np.asarray(inputs["L0_ffn_ln_beta"], f32).reshape(1, 32),
            "gw2": gw2}

    def ffn_b_inputs(rprev):
        cnt_all = np.ascontiguousarray(np.stack(
            [np.asarray(rprev.results[c]["cnt_o"])[0] for c in range(NCORE)]))
        return [{"bst_f": bst_f, "rep_f": rep, "cnt_all": cnt_all,
                 "tmp_i": rprev.results[c]["tmp_o"],
                 "eq_i": rprev.results[c]["eq_o"],
                 "w12_i": rprev.results[c]["w12_o"]} for c in range(NCORE)]

    r3 = run("L3", [dict(ffnw) for _ in range(NCORE)])
    xcur = [np.ascontiguousarray(r3.results[c]["x_o"]) for c in range(NCORE)]
    r4 = run("L4", ffn_b_inputs(r3))
    xcur = [np.ascontiguousarray(r4.results[c]["x_o"]) for c in range(NCORE)]
    xn_all = np.ascontiguousarray(np.concatenate(
        [np.asarray(r4.results[c]["xn_o"]) for c in range(NCORE)], axis=0))
    ex = [{"wq_b": np.ascontiguousarray(wq_b[4 * c:4 * c + 4]),
           "wk_b": np.ascontiguousarray(wk_b[c:c + 1]),
           "wv_b": np.ascontiguousarray(wv_b[c:c + 1]),
           "bq_b": np.ascontiguousarray(bq[512 * c:512 * (c + 1)]).reshape(512, 1),
           "bk_b": np.ascontiguousarray(bk[128 * c:128 * (c + 1)]).reshape(128, 1),
           "bv_b": np.ascontiguousarray(bv[128 * c:128 * (c + 1)]).reshape(128, 1),
           "bcs": cos_bf, "bsn": sin_bf, "dmask": dm, "xn_all": xn_all}
          for c in range(NCORE)]
    r5 = run("L5", ex)
    ctxs = [np.asarray(r5.results[w]["ctx_o"]) for w in range(NCORE)]
    ex = [dict(ffnw, wo_b=wo_b,
               bo_b=np.asarray(inputs["L0_attn_b_bo"], f32).reshape(DM, 1),
               ctx_all=np.ascontiguousarray(np.concatenate(
                   [ctxs[w][c * 512:(c + 1) * 512, :] for w in range(NCORE)],
                   axis=0)))
          for c in range(NCORE)]
    r6 = run("L6", ex)
    xcur = [np.ascontiguousarray(r6.results[c]["x_o"]) for c in range(NCORE)]
    r7 = run("L7", ffn_b_inputs(r6))
    out = np.empty((1, SEQ, DM), f32)
    for c in range(NCORE):
        out[0, c * TLOC:(c + 1) * TLOC, :] = np.asarray(r7.results[c]["out"]).T
    return out
